# version 15 (final)
# baseline (speedup 1.0000x reference)
from contextlib import ExitStack
from dataclasses import dataclass

import numpy as np
import ml_dtypes

import concourse.bass as bass
import concourse.mybir as mybir
from concourse.bass_utils import run_bass_kernel_spmd
from concourse.alu_op_type import AluOpType as OP
F32 = mybir.dt.float32
BF16 = mybir.dt.bfloat16
AFT = mybir.ActivationFunctionType
AX = mybir.AxisListType

NCORES = 8
P = 128
EPS = 1e-5


@dataclass
class Cfg:
    D: int = 2048
    T: int = 8192
    H: int = 16
    HD: int = 128
    INTER: int = 8192
    QT: int = 256

    @property
    def TLOC(self): return self.T // NCORES
    @property
    def DCH(self): return self.D // P
    @property
    def ICH(self): return self.INTER // P
    @property
    def GT(self): return self.T // 8
    @property
    def NTG(self): return self.GT // P
    @property
    def NTO(self): return self.TLOC // P
    @property
    def NKC(self): return self.T // P
    @property
    def NQT(self): return self.TLOC // self.QT
    @property
    def COGS(self): return [(o, min(512, self.D - o)) for o in range(0, self.D, 512)]
    @property
    def TLB(self): return [(o, min(512, self.TLOC - o))
                           for o in range(0, self.TLOC, 512)]


FULL = Cfg()
MINI = Cfg(D=256, T=2048, H=2, HD=128, INTER=1024, QT=256)

SCL = float(np.float32(128.0 ** -0.25))
TWO_PI = float(np.float64(2 * np.pi))
INV_2PI = float(np.float32(1.0 / (2 * np.pi)))
_c1 = np.float32(6.28125)
_c2 = np.float32(np.float64(TWO_PI) - np.float64(_c1))
_c3 = np.float32(np.float64(TWO_PI) - np.float64(_c1) - np.float64(_c2))
C1, C2, C3 = float(_c1), float(_c2), float(_c3)
PI = float(np.float32(np.pi))
PI_2 = float(np.float32(np.pi / 2))


def ref_freqs(hd):
    lin = np.linspace(np.float32(0.0), np.float32(1.0), hd // 2, dtype=np.float32)
    f = np.float32(40.0) ** lin
    f = f * np.float32(200.0) / np.float32(1000.0)
    return (np.float32(200.0) * f / np.float32(1000.0)).astype(np.float32)


class Sem:
    def __init__(self, h):
        self.h, self.n = h, 0

    def inc(self, k=1):
        self.n += k
        return self.n


class Arena:
    def __init__(self, ap, total):
        self.ap, self.total = ap, total
        self.free = [(0, total)]
        self.live = {}

    def alloc(self, name, units):
        units = (units + 7) & ~7
        best = None
        for i, (off, sz) in enumerate(self.free):
            if sz >= units and (best is None or sz < self.free[best][1]):
                best = i
        if best is None:
            raise RuntimeError(f"arena OOM {name} needs {units} free={self.free} "
                               f"live={ {k: v[1] for k, v in self.live.items()} }")
        off, sz = self.free[best]
        self.free[best] = (off + units, sz - units)
        if self.free[best][1] == 0:
            self.free.pop(best)
        self.live[name] = (off, units)
        return self

    def alloc_high(self, name, units):
        units = (units + 7) & ~7
        for i in range(len(self.free) - 1, -1, -1):
            off, sz = self.free[i]
            if sz >= units:
                self.free[i] = (off, sz - units)
                if self.free[i][1] == 0:
                    self.free.pop(i)
                self.live[name] = (off + sz - units, units)
                return self
        raise RuntimeError(f"arena OOM(high) {name} needs {units}")

    def release(self, *names):
        for name in names:
            off, sz = self.live.pop(name)
            self.free.append((off, sz))
        self.free.sort()
        m = []
        for o, s in self.free:
            if m and m[-1][0] + m[-1][1] == o:
                m[-1][1] += s
            else:
                m.append([o, s])
        self.free = [tuple(x) for x in m]

    def f(self, name, cols=None):
        off, sz = self.live[name]
        return self.ap[:, off:off + (sz if cols is None else cols)]

    def b(self, name, cols=None):
        off, sz = self.live[name]
        v = self.ap[:, off:off + sz].bitcast(BF16)
        return v if cols is None else v[:, 0:cols]


def build(cfg: Cfg, dbg: bool = False):
    D, T, H, HD, INTER, QT = cfg.D, cfg.T, cfg.H, cfg.HD, cfg.INTER, cfg.QT
    DCH, ICH, GT, NTG, NTO = cfg.DCH, cfg.ICH, cfg.GT, cfg.NTG, cfg.NTO
    NKC, NQT, TLOC = cfg.NKC, cfg.NQT, cfg.TLOC
    COGS, TLB = cfg.COGS, cfg.TLB
    NCG, NTB = len(COGS), len(TLB)
    NGRP = T // GT

    nc = bass.Bass("TRN2", target_bir_lowering=False)

    x_d = nc.dram_tensor("x_full", [T, D], F32, kind="ExternalInput").ap()
    xow_d = nc.dram_tensor("x_own", [TLOC, D], F32, kind="ExternalInput").ap()
    wq_d = nc.dram_tensor("Wq", [D, D], F32, kind="ExternalInput").ap()
    wkv_d = nc.dram_tensor("Wkv", [D, 2 * D], F32, kind="ExternalInput").ap()
    wo_d = nc.dram_tensor("Wo", [D, D], F32, kind="ExternalInput").ap()
    wgu_d = nc.dram_tensor("Wgu", [2 * INTER, D], F32, kind="ExternalInput").ap()
    wdn_d = nc.dram_tensor("Wdn", [D, INTER], F32, kind="ExternalInput").ap()
    V = {}
    for nm, n in [("ln_q_g", D), ("ln_q_b", D), ("ln_kv_g", D), ("ln_kv_b", D),
                  ("ln_h_g", HD), ("ln_h_b", HD), ("bq", D), ("bkv", 2 * D),
                  ("bo", D)]:
        V[nm] = nc.dram_tensor(nm, [P, n], F32, kind="ExternalInput").ap()
    ident_d = nc.dram_tensor("ident_c", [P, P], BF16, kind="ExternalInput").ap()
    ones_d = nc.dram_tensor("ones_c", [P, 1], BF16, kind="ExternalInput").ap()
    onesf_d = nc.dram_tensor("onesf_c", [1, P], F32, kind="ExternalInput").ap()
    onesbr_d = nc.dram_tensor("onesbr_c", [1, P], BF16, kind="ExternalInput").ap()
    fdup_d = nc.dram_tensor("fdup_c", [P, HD], F32, kind="ExternalInput").ap()
    tcols_d = nc.dram_tensor("tcols_c", [P, T // P], F32, kind="ExternalInput").ap()
    tco_d = nc.dram_tensor("tcown_c", [P, NTO], F32, kind="ExternalInput").ap()
    kq_d = nc.dram_tensor("kq_c", [P, (T // P) * HD], F32,
                          kind="ExternalInput").ap()
    kq2_d = nc.dram_tensor("kq2_c", [P, (T // P) * HD], F32,
                           kind="ExternalInput").ap()
    kqo_d = nc.dram_tensor("kqo_c", [P, NTO * HD], F32,
                           kind="ExternalInput").ap()
    kqo2_d = nc.dram_tensor("kqo2_c", [P, NTO * HD], F32,
                            kind="ExternalInput").ap()
    gqc_d = nc.dram_tensor("gqc_c", [P, DCH], F32, kind="ExternalInput").ap()
    bqc_d = nc.dram_tensor("bqc_c", [P, DCH], F32, kind="ExternalInput").ap()
    gkc_d = nc.dram_tensor("gkc_c", [P, DCH], F32, kind="ExternalInput").ap()
    bkc_d = nc.dram_tensor("bkc_c", [P, DCH], F32, kind="ExternalInput").ap()
    kqk1_d = nc.dram_tensor("kqk1_c", [P, T], F32, kind="ExternalInput").ap()
    kqk2_d = nc.dram_tensor("kqk2_c", [P, T], F32, kind="ExternalInput").ap()
    kqq1_d = nc.dram_tensor("kqq1_c", [P, TLOC], F32, kind="ExternalInput").ap()
    kqq2_d = nc.dram_tensor("kqq2_c", [P, TLOC], F32, kind="ExternalInput").ap()
    tpos_d = nc.dram_tensor("tpos_c", [1, T], F32, kind="ExternalInput").ap()
    tpow_d = nc.dram_tensor("tpow_c", [1, TLOC], F32, kind="ExternalInput").ap()
    frow_d = nc.dram_tensor("frow_c", [1, P], F32, kind="ExternalInput").ap()
    sgnc_d = nc.dram_tensor("sgnc_c", [P, 1], F32, kind="ExternalInput").ap()
    onescf_d = nc.dram_tensor("onescf_c", [P, 1], F32, kind="ExternalInput").ap()
    identf_d = nc.dram_tensor("identf_c", [P, P], F32, kind="ExternalInput").ap()
    pswap_d = nc.dram_tensor("pswap_c", [P, P], F32, kind="ExternalInput").ap()
    hgc_d = nc.dram_tensor("hgc_c", [P, 1], F32, kind="ExternalInput").ap()
    hbc_d = nc.dram_tensor("hbc_c", [P, 1], F32, kind="ExternalInput").ap()
    y_d = nc.dram_tensor("y", [TLOC, D], F32, kind="ExternalOutput").ap()

    wq_bf = nc.dram_tensor("wq_bf", [D, D], BF16).ap()
    wkv_bf = nc.dram_tensor("wkv_bf", [D, 2 * D], BF16).ap()
    wgu_bf = nc.dram_tensor("wgu_bf", [2 * INTER, D], BF16).ap()
    wdn_bf = nc.dram_tensor("wdn_bf", [D, INTER], BF16).ap()
    wguT = nc.dram_tensor("wguT", [2 * ICH * D, P], BF16).ap()
    wdnT = nc.dram_tensor("wdnT", [ICH * P, D], BF16).ap()
    st_kn = nc.dram_tensor("st_kn", [D, T], BF16).ap()
    st_v = nc.dram_tensor("st_v", [H * NKC * P, HD], BF16).ap()
    qn_sp = nc.dram_tensor("qn_sp", [D, TLOC], BF16).ap()
    xn_sp = nc.dram_tensor("xn_sp", [D, TLOC], BF16).ap()
    ffn_sp = nc.dram_tensor("ffn_sp", [TLOC, D], F32).ap()

    ctx = ExitStack()
    AR_U = 50736
    arena_t = ctx.enter_context(nc.sbuf_tensor("arena", [P, AR_U], F32))
    A = Arena(arena_t[:], AR_U)
    small_t = ctx.enter_context(nc.sbuf_tensor("small", [P, 368], F32))
    SM = small_t[:]
    psum_t = ctx.enter_context(nc.psum_tensor("psum", [P, 4096], F32))
    PS, PSB = psum_t[:], psum_t[:].bitcast(BF16)
    ident_t = ctx.enter_context(nc.sbuf_tensor("identS", [P, P], BF16))
    IDENT = ident_t[:]
    ones_t = ctx.enter_context(nc.sbuf_tensor("onesS", [P, 1], BF16))
    ONES = ones_t[:]
    onesf_t = ctx.enter_context(nc.sbuf_tensor("onesF", [1, P], F32))
    ONESF = onesf_t[:]
    trt_t = ctx.enter_context(nc.sbuf_tensor("trtS", [P, 1024], BF16,
                                             is_transpose_dest=True))
    TRT = trt_t[:]
    onesbr_t = ctx.enter_context(nc.sbuf_tensor("onesbrS", [1, P], BF16))
    ONESBR = onesbr_t[:]
    frow_t = ctx.enter_context(nc.sbuf_tensor("frowS", [1, P], F32))
    FROWT = frow_t[:]
    identf_t = ctx.enter_context(nc.sbuf_tensor("identfS", [P, P], F32))
    IDENTF = identf_t[:]
    pswap_t = ctx.enter_context(nc.sbuf_tensor("pswapS", [P, P], F32))
    PSWAPT = pswap_t[:]
    onesr_t = ctx.enter_context(nc.sbuf_tensor("onesrS", [P, P], F32))
    ONESRB = onesr_t[:]

    def bank(b, w=512, o=0):
        return PS[:, b * 512 + o: b * 512 + o + w]

    def bankb(b, w=1024, o=0):
        return PSB[:, b * 1024 + o: b * 1024 + o + w]

    sems = {n: Sem(ctx.enter_context(nc.semaphore(n))) for n in
            ("sx", "stw", "srd", "swgp", "swff", "ste", "sact", "sdve", "sgp",
             "swcast", "swwq", "swkv", "swwo", "sptw", "skq", "swgp2",
             "sx2", "skq2")}
    sx, stw, srd, swgp, swff = (sems[n] for n in ("sx", "stw", "srd", "swgp",
                                                  "swff"))
    s_te, s_act, s_dve, s_gp = (sems[n] for n in ("ste", "sact", "sdve", "sgp"))
    swcast, swwq, swkv, swwo, sptw, skq, swgp2 = (sems[n] for n in
                                                  ("swcast", "swwq", "swkv",
                                                   "swwo", "sptw", "skq",
                                                   "swgp2"))
    sxp = [sx, sems["sx2"]]
    skqp = [skq, sems["skq2"]]

    Q = {k: [] for k in ("te", "act", "dve", "sp", "gp")}

    def emit(q, meth, *args, waits=(), inc=None, inc16=None, **kw):
        r = None
        incs = []
        if inc is not None:
            r = inc.inc()
            incs.append((inc.h, 1))
        if inc16 is not None:
            r = inc16.inc(16)
            incs.append((inc16.h, 16))
        ws = [(s.h, v) for (s, v) in waits if s is not None and v and v > 0]
        Q[q].append((meth, args, kw, ws, incs))
        return r

    def drain(q):
        emit(q, "drain")

    def dma(q, sem, dst, src, waits=()):
        return emit(q, "dma_start", waits=waits, inc16=sem, out=dst, in_=src)

    def dma_tr(q, sem, dst, src, waits=()):
        return emit(q, "dma_start_transpose", dst, src, waits=waits, inc16=sem)

    bg = [None] * 8

    def bgw(b):
        return [] if bg[b] is None else [bg[b]]

    _smo = [0]

    def sm(n):
        o = _smo[0]
        _smo[0] += max(n, 1)
        assert _smo[0] <= 368, _smo[0]
        return SM[:, o:o + n]

    ST2 = sm(32)
    RED1 = sm(2)
    BBFr = sm(8)
    NEGB = sm(1)
    NEGPI = sm(1)
    LROW = sm(QT)
    BCOLS = sm(3 * DCH)
    HGB = sm(2)
    SGNC = sm(1)
    ONESCF = sm(1)
    NMS = sm(8)

    BBF = BBFr.bitcast(BF16)

    dma("sp", sx, IDENT, ident_d[:, :])
    dma("sp", sx, ONES, ones_d[:, :])
    dma("sp", sx, ONESF, onesf_d[:, :])
    dma("sp", sx, ONESBR, onesbr_d[:, :])
    dma("sp", sx, FROWT, frow_d[:, :])
    dma("sp", sx, IDENTF, identf_d[:, :])
    dma("sp", sx, PSWAPT, pswap_d[:, :])
    dma("sp", sx, SGNC, sgnc_d[:, :])
    dma("sp", sx, ONESCF, onescf_d[:, :])
    dma("sp", sx, HGB[:, 0:1], hgc_d[:, :])
    hl_v = dma("sp", sx, HGB[:, 1:2], hbc_d[:, :])
    emit("dve", "memset", NEGPI, -PI)
    drain("dve")
    emit("dve", "memset", ONESRB, 1.0)
    drain("dve")
    emit("dve", "tensor_scalar", HGB, HGB, SCL, None, OP.mult,
         waits=[(sx, hl_v)])
    drain("dve")

    A.alloc("vrow", 2 * D)
    vrow = A.f("vrow")
    A.alloc("brows", 3 * D // 2)
    BROWS = A.b("brows", 3 * D)
    bc_specs = [("lnhg_e", V["ln_h_g"][:, :], HD, SCL),
                ("lnhb_e", V["ln_h_b"][:, :], HD, SCL)]
    last_bc = 0
    for nm, srcv, n, scale in bc_specs:
        A.alloc(nm, (n + 1) // 2)
        vv = dma("gp", swgp, vrow[:, 0:n], srcv, waits=[(s_dve, last_bc)])
        if scale is None:
            last_bc = emit("dve", "tensor_copy", A.b(nm, n), vrow[:, 0:n],
                           waits=[(swgp, vv)], inc=s_dve)
        else:
            last_bc = emit("dve", "tensor_scalar_mul", A.b(nm, n), vrow[:, 0:n],
                           float(scale), waits=[(swgp, vv)], inc=s_dve)
        drain("dve")
    bc_done = last_bc
    av = emit("act", "activation", vrow[:, 0:HD], A.b("lnhg_e", HD), AFT.Square,
              waits=[(s_dve, last_bc)], inc=s_act)
    emit("dve", "reduce_max", RED1[:, 0:1], vrow[:, 0:HD], axis=AX.X,
         waits=[(s_act, av)])
    drain("dve")
    dv = emit("dve", "nop", inc=s_dve)
    av = emit("act", "activation", vrow[:, 0:HD], A.b("lnhb_e", HD), AFT.Square,
              waits=[(s_dve, dv)], inc=s_act)
    emit("dve", "reduce_sum", RED1[:, 1:2], vrow[:, 0:HD], axis=AX.X,
         waits=[(s_act, av)])
    drain("dve")
    dv = emit("dve", "nop", inc=s_dve)
    av = emit("act", "activation", BBFr[:, 2:4], RED1[:, 0:2], AFT.Sqrt,
              waits=[(s_dve, dv)], inc=s_act)
    emit("dve", "tensor_scalar", BBFr[:, 0:1], BBFr[:, 2:3],
         float(np.sqrt(HD)), None, OP.mult, waits=[(s_act, av)])
    drain("dve")
    emit("dve", "tensor_tensor", BBFr[:, 0:1], BBFr[:, 0:1], BBFr[:, 3:4], OP.add)
    drain("dve")
    emit("dve", "tensor_tensor", BBFr[:, 1:2], BBFr[:, 0:1], BBFr[:, 0:1], OP.mult)
    drain("dve")
    emit("dve", "tensor_scalar", NEGB, BBFr[:, 1:2], -1.0001, None, OP.mult)
    drain("dve")
    negb_done = emit("dve", "nop", inc=s_dve)
    A.release("lnhg_e", "lnhb_e")

    A.alloc("wprep", 2 * D)
    A.alloc("gcolp", 2 * DCH + 4)
    WPREP = A.b("wprep", 4 * D)
    GCOL = A.f("gcolp", 2 * DCH)
    BCOLB = A.f("gcolp")[:, 2 * DCH:2 * DCH + 2].bitcast(BF16)
    gl1 = dma("sp", sx, GCOL[:, 0:DCH], gqc_d[:, :],
              waits=[(sx, sx.n)])
    gl2 = dma("sp", sx, GCOL[:, DCH:2 * DCH], gkc_d[:, :],
              waits=[(sx, sx.n)])
    A.alloc("bcolf", DCH * 2)
    BCOLF = A.f("bcolf", 2 * DCH)
    bl1 = dma("sp", sx, BCOLF[:, 0:DCH], bqc_d[:, :],
              waits=[(sx, sx.n)])
    bl2 = dma("sp", sx, BCOLF[:, DCH:2 * DCH], bkc_d[:, :],
              waits=[(sx, sx.n)])
    emit("dve", "nop", waits=[(sx, sx.n)])
    emit("dve", "tensor_scalar_mul", GCOL[:, 0:DCH], GCOL[:, 0:DCH], SCL)
    drain("dve")
    prep_specs = [("q", wq_d, wq_bf, D, 0), ("kv", wkv_d, wkv_bf, 2 * D, 1)]
    A.alloc("brow", 3 * D)
    BROW = A.f("brow", 3 * D)
    prep_done = {}
    for nm, wsrc, wdst, cols, gi in prep_specs:
        ncog = cols // 512 if cols >= 512 else 1
        cw0 = min(512, cols)
        pg = None
        for ki in range(DCH):
            buf = WPREP[:, (ki % 2) * 2 * D:(ki % 2) * 2 * D + cols]
            w = [(s_dve, negb_done)] if ki == 0 and nm == "q" else []
            w.append((sptw, sptw.n))
            psem = swgp if ki % 2 == 0 else swgp2
            cv = dma("gp", psem, buf, wsrc[ki * P:(ki + 1) * P, :], waits=w)
            bv = emit("dve", "tensor_copy", BCOLB[:, gi:gi + 1],
                      BCOLF[:, gi * DCH + ki:gi * DCH + ki + 1],
                      waits=[(psem, cv)], inc=s_dve)
            drain("dve")
            for cg in range(ncog):
                tvp = emit("te", "matmul",
                           bank(cg % 4 + (4 if cg >= 4 else 0), cw0)[0:1, :],
                           BCOLB[:, gi:gi + 1], buf[:, cg * cw0:(cg + 1) * cw0],
                           start=(ki == 0), stop=(ki == DCH - 1),
                           waits=[(s_dve, bv)] if cg == 0 else [],
                           inc=(s_te if cg == ncog - 1 else None))
            sv = emit("dve", "tensor_scalar_mul", buf, buf,
                      GCOL[:, gi * DCH + ki:gi * DCH + ki + 1],
                      waits=[(s_te, tvp)], inc=s_dve)
            drain("dve")
            if nm == "kv":
                sv = emit("dve", "tensor_scalar_mul", buf[:, 0:D], buf[:, 0:D],
                          SCL, inc=s_dve)
                drain("dve")
            dma("sp", sptw, wdst[ki * P:(ki + 1) * P, :], buf,
                waits=[(s_dve, sv)])
        off = 0 if nm == "q" else D
        for cg in range(ncog):
            emit("dve", "tensor_copy", BROW[0:1, off + cg * cw0: off + (cg + 1) * cw0],
                 bank(cg % 4 + (4 if cg >= 4 else 0), cw0)[0:1, :],
                 waits=[(s_te, tvp)] if cg == 0 else [])
        drain("dve")
        prep_done[nm] = emit("dve", "nop", inc=s_dve)
        for b_ in range(8):
            bg[b_] = (s_dve, prep_done[nm])
    A.alloc("rowt", 2 * D)
    ROWT = A.f("rowt", 2 * D)
    r1 = dma("sp", sx, ROWT[0:1, 0:D], V["bq"][0:1, :],
             waits=[(s_dve, prep_done["kv"]), (sx, sx.n)])
    r2 = dma("sp", sx, ROWT[0:1, D:2 * D], V["bkv"][0:1, 0:D],
             waits=[(sx, sx.n)])
    emit("dve", "nop", waits=[(sx, r2)])
    emit("dve", "tensor_tensor", ROWT[0:1, 0:D], ROWT[0:1, 0:D], BROW[0:1, 0:D],
         OP.add)
    emit("dve", "tensor_tensor", ROWT[0:1, D:2 * D], ROWT[0:1, D:2 * D],
         BROW[0:1, D:2 * D], OP.add)
    drain("dve")
    emit("dve", "tensor_scalar", BROWS[0:1, 0:2 * D], ROWT[0:1, 0:2 * D], SCL,
         None, OP.mult)
    drain("dve")
    bvd = emit("dve", "nop", inc=s_dve)
    r3 = dma("sp", sx, ROWT[0:1, 0:D], V["bkv"][0:1, D:2 * D],
             waits=[(s_dve, bvd), (sx, sx.n)])
    emit("dve", "nop", waits=[(sx, r3)])
    emit("dve", "tensor_tensor", BROWS[0:1, 2 * D:3 * D], ROWT[0:1, 0:D],
         BROW[0:1, 2 * D:3 * D], OP.add)
    drain("dve")
    A.release("rowt")
    prep_all = emit("dve", "nop", inc=s_dve)
    A.release("vrow", "wprep", "gcolp", "bcolf", "brow")
    nbc = 3 * DCH
    mk = prep_all
    for c0 in range(0, nbc, 4):
        cn = min(4, nbc - c0)
        for j in range(cn):
            c = c0 + j
            w0 = []
            if j == 0:
                w0 = bgw(7) + [(s_dve, prep_all)]
            tvb = emit("te", "matmul", bank(7, 1, o=j),
                       BROWS[0:1, c * P:(c + 1) * P], ONESBR[0:1, 0:1],
                       start=True, stop=True, waits=w0,
                       inc=(s_te if j == cn - 1 else None))
        emit("dve", "tensor_copy", BCOLS[:, c0:c0 + cn], bank(7, cn),
             waits=[(s_te, tvb)])
        drain("dve")
        mk = emit("dve", "nop", inc=s_dve)
        bg[7] = (s_dve, mk)
    bcols_done = mk
    A.alloc("brow_v", D // 2)
    BRV = A.b("brow_v", D)
    emit("dve", "tensor_copy", BRV[0:1, :], BROWS[0:1, 2 * D:3 * D],
         waits=[(s_dve, bcols_done)])
    drain("dve")
    bcols_done = emit("dve", "nop", inc=s_dve)
    brows_te = s_te.n
    A.release("brows")

    ch = max(P, 2 * INTER // 8)
    for i in range(0, 2 * INTER, ch):
        dma("gp", swcast, wgu_bf[i:i + ch, :], wgu_d[i:i + ch, :])
    ch2 = max(P, D // 4)
    for i in range(0, D, ch2):
        dma("gp", swcast, wdn_bf[i:i + ch2, :], wdn_d[i:i + ch2, :])
    ffn_cast_done = swcast.n


    GU_ROWS = min(1024, 2 * INTER)
    DN_ROWS = min(1024, D)
    _ptr_tiles = []
    for j0 in range(0, 2 * INTER, GU_ROWS):
        for ki in range(DCH):
            _ptr_tiles.append(("gu", j0, ki))
    for h0 in range(0, D, DN_ROWS):
        for ic in range(ICH):
            _ptr_tiles.append(("dn", h0, ic))

    def ptr_slice(idx, total):
        n = len(_ptr_tiles)
        lo, hi = idx * n // total, (idx + 1) * n // total
        for ti in range(lo, hi):
            kind, a0, cc = _ptr_tiles[ti]
            buf_all = TRT
            rows = GU_ROWS if kind == "gu" else DN_ROWS
            buf = buf_all[:, 0:rows]
            w = [(swcast, ffn_cast_done)] if ti in (lo, lo + 1) else []
            w.append((sptw, sptw.n))
            if kind == "gu":
                src = wgu_bf[a0:a0 + rows, cc * P:(cc + 1) * P]
                tv = dma_tr("sp", swff, buf, src, waits=w)
                nblk = rows // P
                dst = bass.AP(wguT.tensor, ((a0 // P) * D + cc * P) * P,
                              [[P, P], [D * P, nblk], [1, P]])
                s3 = buf.rearrange("p (j c) -> p j c", j=nblk)
                dma("sp", sptw, dst, s3, waits=[(swff, tv)])
            else:
                src = wdn_bf[a0:a0 + rows, cc * P:(cc + 1) * P]
                tv = dma_tr("sp", swff, buf, src, waits=w)
                dma("sp", sptw, wdnT[cc * P:(cc + 1) * P, a0:a0 + rows], buf,
                    waits=[(swff, tv)])

    def rsqrt_ref(dst, src, t0):
        av = emit("act", "activation", t0, src, AFT.Sqrt, inc=s_act)
        emit("dve", "reciprocal", dst, t0, waits=[(s_act, av)])
        drain("dve")
        emit("dve", "tensor_tensor", t0, dst, dst, OP.mult)
        drain("dve")
        emit("dve", "tensor_tensor", t0, t0, src, OP.mult)
        drain("dve")
        emit("dve", "tensor_scalar", t0, t0, -0.5, 1.5, OP.mult, OP.add)
        drain("dve")
        emit("dve", "tensor_tensor", dst, dst, t0, OP.mult)
        drain("dve")

    def ln_stats(par, xt, sq):
        ST = ST2[:, par * 16:(par + 1) * 16]
        emit("dve", "reduce_sum", ST[:, 0:1], xt, axis=AX.X)
        av = emit("act", "activation", sq, xt, AFT.Square,
                  accum_out=ST[:, 1:2], inc=s_act)
        emit("dve", "nop", waits=[(s_act, av)])
        drain("dve")
        emit("dve", "tensor_scalar", ST[:, 2:4], ST[:, 0:2], 1.0 / D, None,
             OP.mult)
        drain("dve")
        emit("dve", "tensor_tensor", ST[:, 15:16], ST[:, 2:3], ST[:, 2:3], OP.mult)
        drain("dve")
        emit("dve", "tensor_tensor", ST[:, 3:4], ST[:, 3:4], ST[:, 15:16],
             OP.subtract)
        drain("dve")
        emit("dve", "tensor_scalar", ST[:, 4:5], ST[:, 3:4], EPS, None, OP.add)
        drain("dve")
        emit("dve", "reciprocal", ST[:, 7:8], ST[:, 4:5])
        drain("dve")
        emit("dve", "tensor_tensor", ST[:, 15:16], ST[:, 3:4], ST[:, 7:8], OP.mult)
        drain("dve")
        emit("dve", "tensor_scalar", ST[:, 5:6], ST[:, 15:16], EPS, None, OP.add)
        emit("dve", "tensor_scalar", ST[:, 6:7], ST[:, 15:16], float(D), None,
             OP.mult)
        drain("dve")
        dv = emit("dve", "nop", inc=s_dve)
        av = emit("act", "activation", ST[:, 8:11], ST[:, 4:7], AFT.Sqrt,
                  waits=[(s_dve, dv)], inc=s_act)
        emit("dve", "tensor_scalar", ST[:, 11:12], ST[:, 10:11], -1.0, None,
             OP.mult, waits=[(s_act, av)])
        emit("dve", "reciprocal", ST[:, 12:14], ST[:, 8:10])
        drain("dve")
        emit("dve", "tensor_tensor", ST[:, 14:16], ST[:, 12:14], ST[:, 12:14],
             OP.mult)
        drain("dve")
        emit("dve", "tensor_tensor", ST[:, 14:16], ST[:, 14:16], ST[:, 4:6],
             OP.mult)
        drain("dve")
        emit("dve", "tensor_scalar", ST[:, 14:16], ST[:, 14:16], -0.5, 1.5,
             OP.mult, OP.add)
        drain("dve")
        emit("dve", "tensor_tensor", ST[:, 12:14], ST[:, 12:14], ST[:, 14:16],
             OP.mult)
        drain("dve")
        emit("dve", "tensor_tensor", ST[:, 14:15], ST[:, 12:13], ST[:, 13:14],
             OP.mult)
        drain("dve")

    kq_guard = [0]

    def _tm_waves(src_bf, nch, te_waits):
        for k0 in range(0, nch, 4):
            kn = min(4, nch - k0)
            pb = 4 + (k0 // 4) % 2
            for kk in range(kn):
                w = list(te_waits) if (k0 == 0 and kk == 0) else []
                if kk == 0:
                    w += bgw(pb)
                tv = emit("te", "transpose", bankb(pb, P, o=kk * P),
                          src_bf[:, (k0 + kk) * P:(k0 + kk + 1) * P], IDENT,
                          waits=w, inc=(s_te if kk == kn - 1 else None))
            yield k0, kn, pb, tv

    def tm_to_fm(src_bf, dst4_fn, nch, te_waits=()):
        last = None
        for k0, kn, pb, tv in _tm_waves(src_bf, nch, te_waits):
            emit("dve", "tensor_copy", dst4_fn(k0, kn),
                 bankb(pb, kn * P).rearrange("p (k c) -> p k c", c=P),
                 waits=[(s_te, tv)])
            drain("dve")
            last = emit("dve", "nop", inc=s_dve)
            bg[pb] = (s_dve, last)
        return last


    KWT = 512
    QWT = min(512, TLOC)
    NKW = T // KWT
    NQW = TLOC // QWT
    KTPW = KWT // P
    QTPW = QWT // P

    A.alloc("wres", DCH * D)
    WRES = A.b("wres", 2 * DCH * D)
    WQB = WRES[:, 0:DCH * D]
    WKB = WQB
    WVB = WRES[:, DCH * D:2 * DCH * D]
    A.alloc("akv_w", DCH * KWT)
    AKV = [A.b("akv_w", 2 * DCH * KWT)[:, i * DCH * KWT:(i + 1) * DCH * KWT]
           for i in range(2)]
    AKV3 = [a.rearrange("p (ki g) -> p ki g", g=KWT) for a in AKV]
    A.alloc("xpipe", D)
    XP1 = A.f("xpipe", D)
    A.alloc("sqt", D // 2)
    TILB = A.b("sqt", D)
    A.alloc("rrt", KWT)
    RRT = A.f("rrt", KWT)
    A.alloc("rit", KWT)
    RIT = A.f("rit", KWT)
    KQW = A.f("sqt", KWT)
    A.alloc("zb", 2 * KWT)
    ZBP = [A.f("zb", 2 * KWT)[:, i * KWT:(i + 1) * KWT] for i in range(2)]
    A.alloc("sqz", 2 * KWT)
    SQZP = [A.f("sqz", 2 * KWT)[:, i * KWT:(i + 1) * KWT] for i in range(2)]
    ZNBP = [A.b("sqz", 4 * KWT)[:, i * 2 * KWT:i * 2 * KWT + KWT]
            for i in range(2)]
    A.alloc("srow", 1536)
    SR = A.f("srow", 1536)
    A.alloc("vtile", D // 2)
    VTB = A.b("vtile", D)
    FMQ = A.b("vtile", D)[:, 0:DCH * P]
    FMQ3 = FMQ.rearrange("p (ki c) -> p ki c", ki=DCH)

    srcw = wq_bf[:, :].rearrange("(ki p) n -> p ki n", p=P)
    dstw = WQB.rearrange("p (ki n) -> p ki n", ki=DCH)
    wq_loaded = 0
    for kk in range(0, DCH, 4):
        kn = min(4, DCH - kk)
        wq_loaded = dma("gp", swwq, dstw[:, kk:kk + kn], srcw[:, kk:kk + kn],
                        waits=[(sptw, sptw.n), (s_dve, bcols_done),
                               (s_te, brows_te)]
                        if kk == 0 else [])
    PREP_SPTW = sptw.n
    srcvh = wkv_bf[:, D:2 * D].rearrange("(ki p) n -> p ki n", p=P)
    dstvh = WVB.rearrange("p (ki n) -> p ki n", ki=DCH)
    for kk in range(0, DCH, 4):
        kn = min(4, DCH - kk)
        dma("gp", swkv, dstvh[:, kk:kk + kn], srcvh[:, kk:kk + kn],
            waits=[(sptw, PREP_SPTW)] if kk == 0 else [])
    ptr_slice(0, NKW + 1)

    chain_end = [bcols_done]
    pctr = [0]
    nms_gate = [0, 0]
    ang_gate = [0, 0]
    KV_LOADED = [0]
    XN1Q = [0]

    def pre_tile_chain(xsrc, nms_col, akvw3, ti, spill_t=None):
        gate = chain_end[0]
        par = pctr[0] % 2
        pctr[0] += 1
        ST = ST2[:, 0:16]
        xw = [(s_dve, gate), (sxp[par], sxp[par].n)]
        if pctr[0] <= 2:
            xw.append((sptw, PREP_SPTW))
        xv = dma("sp", sxp[par], XP1, xsrc, waits=xw)
        emit("dve", "nop", waits=[(sxp[par], xv)])
        emit("act", "nop", waits=[(sxp[par], xv), (s_dve, gate),
                                  (s_dve, kq_guard[0])])
        ln_stats(0, XP1, TILB)
        if spill_t is not None:
            emit("dve", "tensor_scalar", TILB, XP1, ST[:, 2:3], ST[:, 12:13],
                 OP.subtract, OP.mult)
            drain("dve")
            dv = emit("dve", "nop", inc=s_dve)
            fmk = tm_to_fm(TILB, lambda k0, kn: FMQ3[:, k0:k0 + kn], DCH,
                           te_waits=[(s_dve, dv)])
            dstx = bass.AP(xn_sp.tensor, spill_t * P,
                           [[TLOC, P], [P * TLOC, DCH], [1, P]])
            wv = dma("sp", stw, dstx, FMQ3, waits=[(s_dve, fmk), (stw, stw.n)])
            emit("dve", "nop", waits=[(stw, wv)])
        emit("dve", "tensor_scalar", TILB, XP1, ST[:, 2:3], ST[:, 14:15],
             OP.subtract, OP.mult)
        drain("dve")
        emit("dve", "tensor_copy", NMS[:, nms_col:nms_col + 1], ST[:, 11:12],
             waits=[(s_te, nms_gate[1 if nms_col >= 4 else 0])])
        drain("dve")
        dv = emit("dve", "nop", inc=s_dve)
        tm_to_fm(TILB,
                 lambda k0, kn, _t=ti: akvw3[:, k0:k0 + kn, _t * P:(_t + 1) * P],
                 DCH, te_waits=[(s_dve, dv)])
        chain_end[0] = emit("dve", "nop", inc=s_dve)

    def window_tables(wpar, wt, ntile, kqsrc1, kqsrc2, tpsrc, akv_done):
        tpr = RIT[0:1, 0:wt]
        mrr = SR[64:65, 0:wt]
        omr = ONESRB[64:65, :]
        kq1 = dma("sp", skq, KQW[:, 0:wt], kqsrc1,
                  waits=[(s_dve, kq_guard[0]), (s_dve, akv_done),
                         (skq, skq.n)])
        tp = dma("sp", skq, tpr, tpsrc,
                 waits=[(s_dve, s_dve.n), (skq, skq.n)])
        tva = emit("te", "matmul", bank(7, wt), FROWT[0:1, :], tpr,
                   start=True, stop=True, waits=bgw(7) + [(skq, tp)],
                   inc=s_te)
        ang_gate[wpar] = tva

        def wrapw(dst, shift, kk, kwv):
            emit("dve", "scalar_tensor_tensor", dst, kk, -C1, bank(7, wt),
                 OP.mult, OP.add, waits=[(s_te, tva), (skq, kwv)])
            drain("dve")
            if shift != 0.0:
                emit("dve", "tensor_scalar", dst, dst, shift, None, OP.add)
                drain("dve")
            emit("dve", "scalar_tensor_tensor", dst, kk, -C2, dst, OP.mult,
                 OP.add)
            drain("dve")
            emit("dve", "scalar_tensor_tensor", dst, kk, -C3, dst, OP.mult,
                 OP.add)
            drain("dve")
            emit("dve", "tensor_scalar", dst, dst, 0.0, TWO_PI, OP.max, OP.min)
            drain("dve")

        wrapw(RIT[:, 0:wt], 0.0, KQW[:, 0:wt], kq1)
        dvk1 = emit("dve", "nop", inc=s_dve)
        kq2 = dma("sp", skq, KQW[:, 0:wt], kqsrc2,
                  waits=[(s_dve, dvk1), (skq, skq.n)])
        wrapw(RRT[:, 0:wt], PI_2, KQW[:, 0:wt], kq2)
        dvk = emit("dve", "nop", inc=s_dve)
        kq_guard[0] = dvk
        bg[7] = (s_dve, dvk)
        a1 = emit("act", "activation", RIT[:, 0:wt], RIT[:, 0:wt], AFT.Sin,
                  bias=NEGPI, scale=1.0, waits=[(s_dve, dvk)], inc=s_act)
        a2 = emit("act", "activation", RRT[:, 0:wt], RRT[:, 0:wt], AFT.Sin,
                  bias=NEGPI, scale=1.0, inc=s_act)
        tvm = None
        for ti in range(ntile):
            col = wpar * 4 + ti
            tvm = emit("te", "matmul", bank(3, P, o=ti * P)[0:1, :],
                       NMS[:, col:col + 1], IDENTF, start=True, stop=True,
                       waits=(bgw(3) + [(s_dve, akv_done)]) if ti == 0 else [],
                       inc=(s_te if ti == ntile - 1 else None))
        nms_gate[wpar] = tvm
        emit("dve", "tensor_copy", mrr, bank(3, wt)[0:1, :], waits=[(s_te, tvm)])
        drain("dve")
        dvm2 = emit("dve", "nop", inc=s_dve)
        bg[3] = (s_dve, dvm2)
        tvmb = emit("te", "matmul", bank(7, wt), omr, mrr,
                    start=True, stop=True, waits=bgw(7) + [(s_dve, dvm2)],
                    inc=s_te)
        emit("dve", "tensor_tensor", RRT[:, 0:wt], RRT[:, 0:wt], bank(7, wt),
             OP.mult, waits=[(s_te, tvmb), (s_act, a2)])
        emit("dve", "tensor_tensor", RIT[:, 0:wt], RIT[:, 0:wt], bank(7, wt),
             OP.mult, waits=[(s_act, a1)])
        drain("dve")
        emit("dve", "tensor_scalar_mul", RIT[:, 0:wt], RIT[:, 0:wt], SGNC)
        drain("dve")
        tbl = emit("dve", "nop", inc=s_dve)
        bg[7] = (s_dve, tbl)

    unit_end = [bcols_done, bcols_done]
    uctr = [0]

    def fm_unit(h, wt, akvw, wb, dstap, bcol, first_w=()):
        par = uctr[0] % 2
        uctr[0] += 1
        gate = unit_end[par]
        z, sqz, znb = ZBP[par], SQZP[par], ZNBP[par]
        zw, sqw, znw = z[:, 0:wt], sqz[:, 0:wt], znb[:, 0:wt]
        def sl(b, blk):
            return SR[b:b + 1, blk * 512:blk * 512 + wt]
        R1, R3, r2 = sl(0, 0), sl(0, 1), sl(0, 2)
        r5, R0 = sl(32, 0), sl(96, 0)
        o2, o5 = ONESRB[0:1, :], ONESRB[32:33, :]

        for ki in range(DCH):
            w0 = list(first_w) if ki == 0 else []
            if ki == 0:
                w0 += bgw(par)
            emit("te", "ldweights",
                 wb[:, ki * D + h * HD:ki * D + (h + 1) * HD], waits=w0)
            tvp = emit("te", "matmul", bank(par, wt),
                       wb[:, ki * D + h * HD:ki * D + (h + 1) * HD],
                       akvw[:, ki * KWT:ki * KWT + wt],
                       start=(ki == 0), stop=(ki == DCH - 1),
                       inc=(s_te if ki == DCH - 1 else None))
        av = emit("act", "activation", zw, bank(par, wt), AFT.Identity,
                  bias=BCOLS[:, bcol:bcol + 1], scale=1.0,
                  waits=[(s_te, tvp), (s_dve, gate)], inc=s_act)
        bg[par] = (s_act, av)
        tvs = emit("te", "matmul", bank(2, wt), PSWAPT, zw, start=True,
                   stop=True, waits=[(s_act, av)] + bgw(2), inc=s_te)
        emit("dve", "tensor_tensor", sqw, bank(2, wt), RIT[:, 0:wt], OP.mult,
             waits=[(s_te, tvs)])
        drain("dve")
        emit("dve", "tensor_tensor", zw, zw, RRT[:, 0:wt], OP.mult)
        drain("dve")
        emit("dve", "tensor_tensor", zw, zw, sqw, OP.add)
        drain("dve")
        dvm = emit("dve", "nop", inc=s_dve)
        bg[2] = (s_dve, dvm)
        avq = emit("act", "activation", sqw, zw, AFT.Square,
                   waits=[(s_dve, dvm)], inc=s_act)
        tv1 = emit("te", "matmul", bank(3, wt)[0:1, :], ONESCF, zw,
                   start=True, stop=True, waits=[(s_dve, dvm)] + bgw(3),
                   inc=s_te)
        tv2 = emit("te", "matmul", bank(7, wt)[0:1, :], ONESCF, sqw,
                   start=True, stop=True, waits=[(s_act, avq)] + bgw(7),
                   inc=s_te)
        emit("dve", "tensor_scalar", R0, bank(3, wt)[0:1, :], 1.0 / HD, None,
             OP.mult, waits=[(s_te, tv1)])
        drain("dve")
        emit("dve", "scalar_tensor_tensor", R3, R0, -1.0, R0, OP.mult, OP.mult)
        drain("dve")
        emit("dve", "scalar_tensor_tensor", R1, bank(7, wt)[0:1, :], 1.0 / HD,
             R3, OP.mult, OP.add, waits=[(s_te, tv2)])
        drain("dve")
        emit("dve", "tensor_scalar", R1, R1, EPS, None, OP.add)
        drain("dve")
        dvp = emit("dve", "nop", inc=s_dve)
        bg[7] = (s_dve, dvp)
        emit("act", "nop", waits=[(s_dve, dvp)])
        rsqrt_ref(r2, R1, R3)
        emit("dve", "scalar_tensor_tensor", r5, bank(3, wt)[0:1, :],
             -1.0 / HD, r2, OP.mult, OP.mult)
        drain("dve")
        dv3 = emit("dve", "nop", inc=s_dve)
        bg[3] = (s_dve, dv3)
        tvb1 = emit("te", "matmul", bank(6, wt), o2, r2,
                    start=True, stop=True, waits=[(s_dve, dv3)] + bgw(6),
                    inc=s_te)
        tvb2 = emit("te", "matmul", bank(3, wt), o5, r5,
                    start=True, stop=True, waits=bgw(3), inc=s_te)
        emit("dve", "tensor_tensor", zw, zw, bank(6, wt), OP.mult,
             waits=[(s_te, tvb1)])
        drain("dve")
        emit("dve", "tensor_tensor", zw, zw, bank(3, wt), OP.add,
             waits=[(s_te, tvb2)])
        drain("dve")
        dv4 = emit("dve", "nop", inc=s_dve)
        bg[6] = (s_dve, dv4)
        bg[3] = (s_dve, dv4)
        emit("dve", "tensor_scalar", znw, zw, HGB[:, 0:1], HGB[:, 1:2],
             OP.mult, OP.add)
        drain("dve")
        dv5 = emit("dve", "nop", inc=s_dve)
        sv = dma("sp", stw, dstap, znw, waits=[(s_dve, dv5), (stw, stw.n)])
        unit_end[par] = emit("dve", "nop", waits=[(stw, sv)], inc=s_dve)

    v_gate = [bcols_done]

    def v_unit(gt, akvw, first_w=(), evac_w=()):
        gate = v_gate[0]
        vt = VTB
        ti = gt % KTPW
        for ci, (co, cw) in enumerate(COGS):
            w0 = bgw(ci)
            if ci == 0:
                w0 = w0 + list(first_w)
            emit("te", "matmul", bank(ci, cw), ONESBR,
                 BRV[0:1, co:co + cw],
                 start=True, stop=False, waits=w0)
        for ki in range(DCH):
            emit("te", "ldweights",
                 akvw[:, ki * KWT + ti * P:ki * KWT + (ti + 1) * P])
            for ci, (co, cw) in enumerate(COGS):
                tv = emit("te", "matmul", bank(ci, cw),
                          akvw[:, ki * KWT + ti * P:ki * KWT + (ti + 1) * P],
                          WVB[:, ki * D + co:ki * D + co + cw],
                          start=False, stop=(ki == DCH - 1),
                          inc=(s_te if (ki == DCH - 1 and ci == NCG - 1)
                               else None))
        for ci, (co, cw) in enumerate(COGS):
            w0 = ([(s_te, tv), (s_dve, gate)] + list(evac_w)) if ci == 0 else []
            av = emit("act", "activation", vt[:, co:co + cw], bank(ci, cw),
                      AFT.Identity, waits=w0,
                      inc=(s_act if ci == NCG - 1 else None))
        for ci in range(NCG):
            bg[ci] = (s_act, av)
        dstv = bass.AP(st_v.tensor, gt * P * HD,
                       [[HD, P], [NKC * P * HD, H], [1, HD]])
        vv = dma("sp", stw, dstv, vt.rearrange("p (h c) -> p h c", h=H),
                 waits=[(s_act, av), (stw, stw.n)])
        v_gate[0] = emit("dve", "nop", waits=[(stw, vv)], inc=s_dve)

    def win_pres(kind, w):
        out = []
        npw = QTPW if kind == "q" else KTPW
        for ti in range(npw):
            if kind == "q":
                t = w * QTPW + ti
                out.append(dict(xsrc=xow_d[t * P:(t + 1) * P, :],
                                nms_col=(w % 2) * 4 + ti,
                                akvw3=AKV3[w % 2], ti=ti, spill_t=t))
            else:
                gt = w * KTPW + ti
                out.append(dict(xsrc=x_d[gt * P:(gt + 1) * P, :],
                                nms_col=(w % 2) * 4 + ti,
                                akvw3=AKV3[w % 2], ti=ti, spill_t=None))
        return out

    WINS = [("q", i) for i in range(NQW)] + [("kv", i) for i in range(NKW)]
    q_te_done = 0
    for wi, (kind, w) in enumerate(WINS):
        if wi == 0:
            for kwargs in win_pres(kind, w):
                pre_tile_chain(**kwargs)
        if wi + 1 < len(WINS):
            nk, nw = WINS[wi + 1]
            nxt = win_pres(nk, nw)
            ok_il = (nw % 2) != (w % 2)
        else:
            nxt, ok_il = [], False
        akv_done = chain_end[0]
        wt = QWT if kind == "q" else KWT
        ntile = QTPW if kind == "q" else KTPW
        if kind == "q":
            window_tables(w % 2, wt, ntile,
                          kqq1_d[:, w * QWT:w * QWT + wt],
                          kqq2_d[:, w * QWT:w * QWT + wt],
                          tpow_d[0:1, w * QWT:w * QWT + wt], akv_done)
        else:
            window_tables(w % 2, wt, ntile,
                          kqk1_d[:, w * KWT:w * KWT + wt],
                          kqk2_d[:, w * KWT:w * KWT + wt],
                          tpos_d[0:1, w * KWT:w * KWT + wt], akv_done)
        for h in range(H):
            fw = [(s_dve, akv_done)]
            if kind == "q":
                if wi == 0 and h == 0:
                    fw.append((swwq, wq_loaded))
                fm_unit(h, wt, AKV[w % 2], WQB,
                        qn_sp[h * P:(h + 1) * P, w * QWT:w * QWT + wt],
                        h, first_w=fw)
            else:
                if w == 0 and h == 0:
                    fw.append((swkv, KV_LOADED[0]))
                fm_unit(h, wt, AKV[w % 2], WKB,
                        st_kn[h * P:(h + 1) * P, w * KWT:(w + 1) * KWT],
                        DCH + h, first_w=fw)
            if ok_il and h % 4 == 3 and nxt:
                pre_tile_chain(**nxt.pop(0))
        if kind == "kv":
            for ti in range(KTPW):
                gt = w * KTPW + ti
                fv = [(swkv, KV_LOADED[0])] if (w == 0 and ti == 0) else []
                ev = [(stw, XN1Q[0])] if (w == 0 and ti <= 1) else []
                v_unit(gt, AKV[w % 2], first_w=fv, evac_w=ev)
            ptr_slice(w + 1, NKW + 1)
        for kwargs in nxt:
            pre_tile_chain(**kwargs)
        if kind == "q" and wi == NQW - 1:
            q_te_done = s_te.n
            XN1Q[0] = stw.n
            srckh = wkv_bf[:, 0:D].rearrange("(ki p) n -> p ki n", p=P)
            dstkh = WKB.rearrange("p (ki n) -> p ki n", ki=DCH)
            kvld = 0
            for kk in range(0, DCH, 4):
                kn = min(4, DCH - kk)
                kvld = dma("gp", swkv, dstkh[:, kk:kk + kn],
                           srckh[:, kk:kk + kn],
                           waits=[(s_te, q_te_done), (sptw, PREP_SPTW)]
                           if kk == 0 else [])
            KV_LOADED[0] = kvld

    A.release("wres", "akv_w", "xpipe", "sqt", "rrt", "rit",
              "zb", "sqz", "srow", "vtile", "brow_v")
    kv_all_stored = stw.n

    A.alloc("bo_b", D // 2)
    bo_v = dma("gp", swgp, A.b("bo_b", D), V["bo"][:, :],
               waits=[(swgp, swgp.n), (s_te, s_te.n), (s_dve, s_dve.n),
                      (s_act, s_act.n)])
    A.alloc("xn_fm", DCH * TLOC // 2)
    XNF = A.b("xn_fm", DCH * TLOC)
    srcx = bass.AP(xn_sp.tensor, 0, [[TLOC, P], [P * TLOC, DCH], [1, TLOC]])
    ph1_te_done = s_te.n
    xn_rl = dma("sp", srd, XNF.rearrange("p (ki n) -> p ki n", ki=DCH), srcx,
                waits=[(stw, stw.n), (s_te, ph1_te_done), (s_dve, s_dve.n),
                       (sptw, sptw.n)])
    A.alloc("m_fm", ICH * TLOC // 2)
    MFM = A.b("m_fm", ICH * TLOC)
    A.alloc("wgu_t", 2 * DCH * P)
    GBUF = A.b("wgu_t", 4 * DCH * P)
    A.alloc("silu_t", TLOC)
    SILU = A.f("silu_t", TLOC)

    gu_guard = [None, None]
    for j in range(ICH):
        lv = 0
        for s_i, blk in enumerate((j, ICH + j)):
            dbuf = GBUF[:, ((j % 2) * 2 + s_i) * DCH * P:
                        ((j % 2) * 2 + s_i + 1) * DCH * P]
            srcb = bass.AP(wguT.tensor, blk * D * P,
                           [[P, P], [P * P, DCH], [1, P]])
            w = []
            if s_i == 0:
                w = [(srd, srd.n)]
            if j < 2:
                w += [(sptw, sptw.n), (s_te, ph1_te_done)]
            elif s_i == 0 and gu_guard[j % 2] is not None:
                w += [gu_guard[j % 2]]
            lv = dma("sp", srd, dbuf.rearrange("p (ki c) -> p ki c", ki=DCH),
                     srcb, waits=w)
        emit("te", "nop", waits=[(srd, lv), (srd, xn_rl)])
        for s_i in range(2):
            dbuf = GBUF[:, ((j % 2) * 2 + s_i) * DCH * P:
                        ((j % 2) * 2 + s_i + 1) * DCH * P]
            for ki in range(DCH):
                emit("te", "ldweights", dbuf[:, ki * P:(ki + 1) * P])
                for bi, (to, tw) in enumerate(TLB):
                    bidx = s_i * 2 + bi
                    w = bgw(bidx) if ki == 0 else []
                    tv = emit("te", "matmul", bank(bidx, tw),
                              dbuf[:, ki * P:(ki + 1) * P],
                              XNF[:, ki * TLOC + to: ki * TLOC + to + tw],
                              start=(ki == 0), stop=(ki == DCH - 1), waits=w,
                              inc=(s_te if (s_i == 1 and ki == DCH - 1
                                            and bi == NTB - 1) else None))
        gu_guard[j % 2] = (s_te, tv)
        for bi, (to, tw) in enumerate(TLB):
            av = emit("act", "activation", SILU[:, to:to + tw], bank(bi, tw),
                      AFT.Silu, waits=[(s_te, tv)] if bi == 0 else [],
                      inc=(s_act if bi == NTB - 1 else None))
        for bi, (to, tw) in enumerate(TLB):
            emit("dve", "tensor_tensor",
                 MFM[:, j * TLOC + to: j * TLOC + to + tw],
                 SILU[:, to:to + tw], bank(2 + bi, tw), OP.mult,
                 waits=[(s_act, av)] if bi == 0 else [])
        drain("dve")
        mk = emit("dve", "nop", inc=s_dve)
        for bidx in range(2 * NTB + 2):
            if bidx < 8:
                bg[bidx] = (s_dve, mk)
    A.release("wgu_t", "silu_t", "xn_fm")
    up_te_done = s_te.n
    up_dve_done = s_dve.n

    A.alloc("wdn_t", 2 * D)
    WDN = A.b("wdn_t", 2 * D)
    A.alloc("ffn_t", D)
    FFT = A.f("ffn_t", D)
    dn_guard = [None, None]
    out_stash = 0
    for tp in range(0, NTO, 2):
        ntp = min(2, NTO - tp)
        for i in range(ICH):
            dbuf = WDN[:, (i % 2) * D:(i % 2) * D + D]
            w = [(srd, srd.n)]
            if dn_guard[i % 2] is not None:
                w.append(dn_guard[i % 2])
            if tp == 0 and i < 2:
                w += [(sptw, sptw.n), (s_te, up_te_done), (s_dve, up_dve_done)]
            lv = dma("sp", srd, dbuf, wdnT[i * P:(i + 1) * P, :], waits=w)
            emit("te", "nop", waits=[(srd, lv)])
            for tti in range(ntp):
                tt = tp + tti
                emit("te", "ldweights",
                     MFM[:, i * TLOC + tt * P: i * TLOC + (tt + 1) * P])
                for ci, (co, cw) in enumerate(COGS):
                    bidx = tti * NCG + ci
                    w2 = bgw(bidx) if i == 0 else []
                    tvd = emit("te", "matmul", bank(bidx, cw),
                               MFM[:, i * TLOC + tt * P: i * TLOC + (tt + 1) * P],
                               WDN[:, (i % 2) * D + co:(i % 2) * D + co + cw],
                               start=(i == 0), stop=(i == ICH - 1), waits=w2,
                               inc=(s_te if (tti == ntp - 1 and ci == NCG - 1)
                                    else None))
            dn_guard[i % 2] = (s_te, tvd)
        tv = dn_guard[(ICH - 1) % 2][1]
        for tti in range(ntp):
            tt = tp + tti
            for ci, (co, cw) in enumerate(COGS):
                bidx = tti * NCG + ci
                emit("dve", "scalar_tensor_tensor", FFT[:, co:co + cw],
                     bank(bidx, cw), 1.0, A.b("bo_b", D)[:, co:co + cw],
                     OP.mult, OP.add,
                     waits=([(s_te, tv), (stw, out_stash), (swgp, bo_v)]
                            if (tti == 0 and ci == 0) else []))
            drain("dve")
            dv = emit("dve", "nop", inc=s_dve)
            out_stash = dma("sp", stw, ffn_sp[tt * P:(tt + 1) * P, :], FFT,
                            waits=[(s_dve, dv), (stw, stw.n)])
            mk = emit("dve", "nop", waits=[(stw, out_stash)], inc=s_dve)
            for ci in range(NCG):
                bg[tti * NCG + ci] = (s_dve, mk)
    A.release("wdn_t", "ffn_t", "m_fm")

    A.alloc("qn_fm", DCH * TLOC // 2)
    QNF = A.b("qn_fm", DCH * TLOC)
    srcq = bass.AP(qn_sp.tensor, 0, [[TLOC, P], [P * TLOC, DCH], [1, TLOC]])
    ph2_te_done = s_te.n
    qn_rl = dma("sp", srd, QNF.rearrange("p (ki n) -> p ki n", ki=DCH), srcq,
                waits=[(s_te, ph2_te_done), (s_dve, s_dve.n)])
    A.alloc("kn_h", NKC * P)
    KNB = A.b("kn_h", 2 * NKC * P)
    A.alloc("v_h", NKC * HD)
    VB = A.b("v_h", 2 * NKC * HD)
    A.alloc("p_buf", NKC * QT // 2)
    PBUF = A.b("p_buf", NKC * QT)
    A.alloc("attn_fm", DCH * TLOC // 2)
    ATTN = A.b("attn_fm", DCH * TLOC)
    A.alloc("lrec", QT)
    LREC = A.f("lrec", QT)

    kn_loads = [0] * H
    head_done = [None] * (H + 2)

    def load_head(h):
        par = h % 2
        w = [(stw, kv_all_stored), (s_te, ph2_te_done), (srd, srd.n)]
        if h >= 2 and head_done[h - 2] is not None:
            w = [head_done[h - 2], (srd, srd.n)]
        last = dma("sp", srd,
                   KNB[:, par * NKC * P:(par + 1) * NKC * P],
                   st_kn[h * P:(h + 1) * P, :], waits=w)
        srcv = bass.AP(st_v.tensor, h * NKC * P * HD,
                       [[HD, P], [P * HD, NKC], [1, HD]])
        last = dma("sp", srd,
                   VB[:, par * NKC * HD:(par + 1) * NKC * HD]
                   .rearrange("p (kc c) -> p kc c", kc=NKC), srcv)
        kn_loads[h] = last

    load_head(0)
    if H > 1:
        load_head(1)
    emit("act", "nop", waits=[(s_dve, negb_done)])
    for h in range(H):
        par = h % 2
        kn_h = KNB[:, par * NKC * P:(par + 1) * NKC * P]
        v_h = VB[:, par * NKC * HD:(par + 1) * NKC * HD]
        emit("te", "nop", waits=[(srd, kn_loads[h]), (srd, qn_rl),
                                 (s_dve, negb_done)])
        for qi in range(NQT):
            q0 = qi * QT
            rhs_q = QNF[:, h * TLOC + q0: h * TLOC + q0 + QT]
            exp_marks = []
            QKB = (0, 1, 4, 5)
            for c0 in range(0, NKC, 2):
                b_i = QKB[(c0 // 2) % 4]
                npair = min(2, NKC - c0)
                for cc in range(npair):
                    w = bgw(b_i) if cc == 0 else []
                    emit("te", "ldweights",
                         kn_h[:, (c0 + cc) * P:(c0 + cc + 1) * P])
                    tv = emit("te", "matmul", bank(b_i, QT, o=cc * QT),
                              kn_h[:, (c0 + cc) * P:(c0 + cc + 1) * P], rhs_q,
                              start=True, stop=True, waits=w,
                              inc=(s_te if cc == npair - 1 else None))
                av = emit("act", "activation",
                          PBUF[:, c0 * QT:(c0 + npair) * QT],
                          bank(b_i, npair * QT), AFT.Exp, bias=NEGB, scale=1.0,
                          waits=[(s_te, tv)], inc=s_act)
                exp_marks.append(av)
                bg[b_i] = (s_act, av)
            for c in range(NKC):
                w = [(s_act, exp_marks[c // 2])]
                if c == 0:
                    w += bgw(2)
                emit("te", "ldweights", v_h[:, c * HD:(c + 1) * HD], waits=w)
                av_tv = emit("te", "matmul", bank(2, QT),
                             v_h[:, c * HD:(c + 1) * HD],
                             PBUF[:, c * QT:(c + 1) * QT],
                             start=(c == 0), stop=(c == NKC - 1),
                             inc=(s_te if c == NKC - 1 else None))
            emit("te", "ldweights", ONES, waits=bgw(3))
            for c in range(NKC):
                l_tv = emit("te", "matmul", bank(3, QT)[0:1, :], ONES,
                            PBUF[:, c * QT:(c + 1) * QT],
                            start=(c == 0), stop=(c == NKC - 1),
                            inc=(s_te if c == NKC - 1 else None))
            emit("dve", "tensor_copy", LROW[0:1, :], bank(3, QT)[0:1, :],
                 waits=[(s_te, l_tv)])
            drain("dve")
            emit("dve", "reciprocal", LROW[0:1, :], LROW[0:1, :])
            drain("dve")
            dv = emit("dve", "nop", inc=s_dve)
            bg[3] = (s_dve, dv)
            tv = emit("te", "matmul", bank(6, QT), ONESF[0:1, :],
                      LROW[0:1, :], start=True, stop=True,
                      waits=[(s_dve, dv)] + bgw(6), inc=s_te)
            emit("dve", "tensor_copy", LREC, bank(6, QT), waits=[(s_te, tv)])
            drain("dve")
            lb = emit("dve", "nop", inc=s_dve)
            bg[6] = (s_dve, lb)
            emit("dve", "tensor_tensor",
                 ATTN[:, h * TLOC + q0: h * TLOC + q0 + QT], bank(2, QT), LREC,
                 OP.mult, waits=[(s_te, av_tv)])
            drain("dve")
            bg[2] = (s_dve, emit("dve", "nop", inc=s_dve))
        head_done[h] = (s_te, l_tv)
        if h + 2 < H:
            load_head(h + 2)
    attn_done = emit("dve", "nop", inc=s_dve)
    A.release("kn_h", "v_h", "p_buf", "qn_fm", "lrec")

    A.alloc("wo_r", DCH * D // 2)
    WOB = A.b("wo_r", DCH * D)
    srcw2 = wo_d[:, :].rearrange("(ki p) n -> p ki n", p=P)
    dstw2 = WOB.rearrange("p (ki n) -> p ki n", ki=DCH)
    wo_loaded = 0
    for kk in range(0, DCH, 4):
        kn = min(4, DCH - kk)
        wo_loaded = dma("gp", swwo, dstw2[:, kk:kk + kn], srcw2[:, kk:kk + kn],
                        waits=[(s_te, head_done[H - 1][1] if head_done[H - 1]
                                else 0), (s_dve, attn_done)] if kk == 0 else [])
    A.alloc("ffn_in", D)
    FIN = A.f("ffn_in", D)
    A.alloc("x_in", D)
    XIN = A.f("x_in", D)
    A.alloc("out_t", D)
    OUTT = A.f("out_t", D)
    emit("te", "nop", waits=[(swwo, wo_loaded), (s_dve, attn_done)])
    out_guard = 0
    for tt in range(NTO):
        fv = dma("sp", srd, FIN, ffn_sp[tt * P:(tt + 1) * P, :],
                 waits=[(s_dve, out_guard if tt > 0 else attn_done)])
        xv = dma("sp", srd, XIN, xow_d[tt * P:(tt + 1) * P, :])
        for ki in range(DCH):
            emit("te", "ldweights",
                 ATTN[:, ki * TLOC + tt * P: ki * TLOC + (tt + 1) * P])
            for ci, (co, cw) in enumerate(COGS):
                w = bgw(ci) if ki == 0 else []
                tv = emit("te", "matmul", bank(ci, cw),
                          ATTN[:, ki * TLOC + tt * P: ki * TLOC + (tt + 1) * P],
                          WOB[:, ki * D + co: ki * D + co + cw],
                          start=(ki == 0), stop=(ki == DCH - 1), waits=w,
                          inc=(s_te if (ki == DCH - 1 and ci == NCG - 1)
                               else None))
        for ci, (co, cw) in enumerate(COGS):
            w = [(s_te, tv), (srd, xv)] if ci == 0 else []
            emit("dve", "tensor_tensor", OUTT[:, co:co + cw], bank(ci, cw),
                 FIN[:, co:co + cw], OP.add, waits=w)
        drain("dve")
        emit("dve", "tensor_tensor", OUTT, OUTT, XIN, OP.add,
             waits=[(srd, xv)])
        drain("dve")
        dv = emit("dve", "nop", inc=s_dve)
        for ci in range(NCG):
            bg[ci] = (s_dve, dv)
        ov = dma("sp", stw, y_d[tt * P:(tt + 1) * P, :], OUTT,
                 waits=[(s_dve, dv), (stw, stw.n)])
        out_guard = emit("dve", "nop", waits=[(stw, ov)], inc=s_dve)

    if dbg:
        dbgs = [("d_xn", xn_sp[:, :]), ("d_qn", qn_sp[:, :]),
                ("d_kn", st_kn[0:D, :]), ("d_v", st_v[0:NKC * P, :]),
                ("d_ffn", ffn_sp[:, :]), ("d_wgu", wguT[0:D, :]),
                ("d_wdn", wdnT[0:P, :])]
        for nm, srcd in dbgs:
            od = nc.dram_tensor(nm, list(srcd.shape),
                                srcd.dtype, kind="ExternalOutput").ap()
            dma("gp", swgp, od[:, :], srcd,
                waits=[(stw, stw.n), (sptw, sptw.n), (srd, srd.n)])
        emit("gp", "nop", waits=[(swgp, swgp.n)])
    emit("sp", "nop", waits=[(stw, stw.n)])
    emit("te", "nop", waits=[(stw, stw.n)])
    emit("gp", "nop", waits=[(stw, stw.n)])
    emit("act", "nop", waits=[(stw, stw.n)])

    def render(q):
        def body(eng):
            for meth, args, kw, waits, incs in Q[q]:
                for s, v in waits:
                    eng.wait_ge(s, v)
                if meth == "nop":
                    if incs:
                        ins = eng.nop()
                        for s, k in incs:
                            ins.then_inc(s, k)
                    continue
                ins = getattr(eng, meth)(*args, **kw)
                if ins is not None:
                    for s, k in incs:
                        ins.then_inc(s, k)
        return body

    with nc.Block() as blk:
        blk.tensor(render("te"))
        blk.scalar(render("act"))
        blk.vector(render("dve"))
        blk.sync(render("sp"))
        blk.gpsimd(render("gp"))
    ctx.close()
    return nc


def _consts(cfg: Cfg):
    f = ref_freqs(cfg.HD)
    fdup = np.repeat(f, 2)[None, :].repeat(P, 0).astype(np.float32)
    tcols = (np.arange(cfg.T, dtype=np.float32).reshape(cfg.T // P, P).T)
    return (np.ascontiguousarray(fdup), np.ascontiguousarray(tcols),
            np.eye(P).astype(ml_dtypes.bfloat16),
            np.ones((P, 1), ml_dtypes.bfloat16))


def _kq_fm(cfg: Cfg, t0, t1):
    f = ref_freqs(cfg.HD)
    fdup = np.repeat(f, 2).astype(np.float32)
    t_idx = np.arange(t0, t1, dtype=np.float32)
    ang32 = (t_idx[:, None] * fdup[None, :]).astype(np.float32)
    k1 = np.floor(ang32.astype(np.float64) / (2 * np.pi))
    k2 = np.floor((ang32.astype(np.float64) + np.float64(np.float32(np.pi / 2)))
                  / (2 * np.pi))
    return (np.ascontiguousarray(k1.astype(np.float32).T),
            np.ascontiguousarray(k2.astype(np.float32).T))


def _kq_tables(cfg: Cfg, t0, t1):
    f = ref_freqs(cfg.HD)
    fdup = np.repeat(f, 2).astype(np.float32)
    t_idx = np.arange(t0, t1, dtype=np.float32)
    ang32 = (t_idx[:, None] * fdup[None, :]).astype(np.float32)
    k1 = np.floor(ang32.astype(np.float64) / (2 * np.pi))
    k2 = np.floor((ang32.astype(np.float64) + np.float64(np.float32(np.pi / 2)))
                  / (2 * np.pi))
    nt = (t1 - t0) // P

    def lay(k):
        k = k.astype(np.float32).reshape(nt, P, cfg.HD)
        return np.ascontiguousarray(k.transpose(1, 0, 2).reshape(P, nt * cfg.HD))

    return lay(k1), lay(k2)


def _bc(v):
    return np.ascontiguousarray(
        np.broadcast_to(np.asarray(v, np.float32).reshape(1, -1), (P, len(np.asarray(v).reshape(-1)))))


def run(cfg: Cfg, inputs: dict, runner=None, dbg=False):
    nc = build(cfg, dbg=dbg)
    fdup, tcols, ident, ones = _consts(cfg)
    x = np.ascontiguousarray(np.asarray(inputs["L0_attn_x"], np.float32)
                             .reshape(cfg.T, cfg.D))
    base = {
        "x_full": x,
        "Wq": np.ascontiguousarray(np.asarray(inputs["L0_attn_Wq"], np.float32)),
        "Wkv": np.ascontiguousarray(np.asarray(inputs["L0_attn_Wkv"], np.float32)),
        "Wo": np.ascontiguousarray(np.asarray(inputs["L0_attn_Wo"], np.float32)),
        "Wgu": np.ascontiguousarray(np.asarray(inputs["L0_ffn_w_gate_up"],
                                               np.float32)),
        "Wdn": np.ascontiguousarray(np.asarray(inputs["L0_ffn_w_down"],
                                               np.float32)),
        "ln_q_g": _bc(inputs["L0_attn_ln_q_g"]),
        "ln_q_b": _bc(inputs["L0_attn_ln_q_b"]),
        "ln_kv_g": _bc(inputs["L0_attn_ln_kv_g"]),
        "ln_kv_b": _bc(inputs["L0_attn_ln_kv_b"]),
        "ln_h_g": _bc(inputs["L0_attn_ln_h_g"]),
        "ln_h_b": _bc(inputs["L0_attn_ln_h_b"]),
        "bq": _bc(inputs["L0_attn_bq"]),
        "bkv": _bc(inputs["L0_attn_bkv"]),
        "bo": _bc(inputs["L0_attn_bo"]),
        "ident_c": ident, "ones_c": ones, "fdup_c": fdup, "tcols_c": tcols,
        "onesf_c": np.ones((1, P), np.float32),
        "onesbr_c": np.ones((1, P), ml_dtypes.bfloat16),
    }
    kq1, kq2 = _kq_tables(cfg, 0, cfg.T)
    base["kq_c"], base["kq2_c"] = kq1, kq2
    base["kqk1_c"], base["kqk2_c"] = _kq_fm(cfg, 0, cfg.T)
    base["tpos_c"] = np.arange(cfg.T, dtype=np.float32)[None, :]
    fdup_r = np.repeat(ref_freqs(cfg.HD), 2).astype(np.float32)
    base["frow_c"] = np.ascontiguousarray(fdup_r[None, :])
    base["sgnc_c"] = np.ascontiguousarray(
        np.tile(np.array([-1.0, 1.0], np.float32), P // 2)[:, None])
    base["onescf_c"] = np.ones((P, 1), np.float32)
    base["identf_c"] = np.eye(P, dtype=np.float32)
    psw = np.zeros((P, P), np.float32)
    for i in range(P):
        psw[i, i ^ 1] = 1.0
    base["pswap_c"] = psw
    base["hgc_c"] = np.ascontiguousarray(
        np.asarray(inputs["L0_attn_ln_h_g"], np.float32).reshape(P, 1))
    base["hbc_c"] = np.ascontiguousarray(
        np.asarray(inputs["L0_attn_ln_h_b"], np.float32).reshape(P, 1))

    def _col(v):
        return np.ascontiguousarray(
            np.asarray(v, np.float32).reshape(cfg.DCH, P).T)
    base["gqc_c"] = _col(inputs["L0_attn_ln_q_g"])
    base["bqc_c"] = _col(inputs["L0_attn_ln_q_b"])
    base["gkc_c"] = _col(inputs["L0_attn_ln_kv_g"])
    base["bkc_c"] = _col(inputs["L0_attn_ln_kv_b"])
    base = {k: np.ascontiguousarray(v) for k, v in base.items()}
    in_maps = []
    tl = cfg.TLOC
    for c in range(NCORES):
        m = dict(base)
        m["x_own"] = np.ascontiguousarray(x[c * tl:(c + 1) * tl, :])
        m["tcown_c"] = np.ascontiguousarray(
            np.arange(c * tl, (c + 1) * tl, dtype=np.float32)
            .reshape(tl // P, P).T)
        m["kqo_c"], m["kqo2_c"] = _kq_tables(cfg, c * tl, (c + 1) * tl)
        m["kqq1_c"], m["kqq2_c"] = _kq_fm(cfg, c * tl, (c + 1) * tl)
        m["tpow_c"] = np.arange(c * tl, (c + 1) * tl,
                                dtype=np.float32)[None, :]
        in_maps.append(m)
    if runner is None:
        res = run_bass_kernel_spmd(nc, in_maps, core_ids=list(range(NCORES)))
        outs = [res.results[c]["y"] for c in range(NCORES)]
        if dbg:
            y = np.concatenate([np.asarray(o, np.float32) for o in outs], axis=0)
            return y.reshape(1, cfg.T, cfg.D), res.results
    else:
        outs = runner(nc, in_maps)
    y = np.concatenate([np.asarray(o, np.float32) for o in outs], axis=0)
    return y.reshape(1, cfg.T, cfg.D)


def kernel(**inputs):
    return run(FULL, inputs)
